# version 06
# baseline (speedup 1.0000x reference)
import os
from contextlib import ExitStack

import numpy as np
import ml_dtypes

import concourse.bass as bass
import concourse.mybir as mybir
from concourse.bass_utils import run_bass_kernel_spmd

F32 = mybir.dt.float32
BF16 = mybir.dt.bfloat16
I32 = mybir.dt.int32
AF = mybir.ActivationFunctionType
ALU = mybir.AluOpType
AX = mybir.AxisListType
BF = ml_dtypes.bfloat16

S = 8192
HID = 2048
NH = 16
D = 128
RATIO = 4
C = S // RATIO
WIN = 128
FFN = 5632
NEG = -3.0e38
EPS_LN = 1e-5
EPS_RMS = 1e-6
SCALE = float(np.float32(1.0) / np.sqrt(np.float32(D)))
PI = float(np.pi)
NCORE = 8
NT = 64
TPC = 8
NPAIR = 4
NSLOT = 12
HG = 4
NHG = NH // HG
PAIR_W = [512, 1024, 1536, 2048]
TILE_W = [PAIR_W[k] for k in range(NPAIR) for _ in range(2)]
KB = 1024
MEGA_BYTES = 190 * KB


def core_pairs(c):
    return [c, 15 - c, 16 + c, 31 - c]


def own_global_tiles(c):
    out = []
    for p in core_pairs(c):
        out += [2 * p, 2 * p + 1]
    return out


def slot_global_tiles(c):
    out = []
    for p in core_pairs(c):
        h = 2 * p - 1
        if h < 0:
            h = 0
        out += [h, 2 * p, 2 * p + 1]
    return out


class Prog:
    def __init__(self, nc):
        self.nc = nc
        self.steps = []
        self.ivals = {}

    def _deps_for(self, reads, writes):
        deps = set()
        for (sp, lo, hi) in list(reads) + list(writes):
            for iv in self.ivals.get(sp, []):
                if iv[0] < hi and lo < iv[1] and iv[2] is not None:
                    deps.add(iv[2])
        for (sp, lo, hi) in writes:
            for iv in self.ivals.get(sp, []):
                if iv[0] < hi and lo < iv[1]:
                    deps.update(iv[3])
        return deps

    def _update(self, sid, reads, writes):
        for (sp, lo, hi) in writes:
            lst = self.ivals.setdefault(sp, [])
            keep = []
            for iv in lst:
                if iv[0] < hi and lo < iv[1]:
                    if iv[0] < lo:
                        keep.append([iv[0], lo, iv[2], set(iv[3])])
                    if hi < iv[1]:
                        keep.append([hi, iv[1], iv[2], set(iv[3])])
                else:
                    keep.append(iv)
            keep.append([lo, hi, sid, set()])
            self.ivals[sp] = keep
        for (sp, lo, hi) in reads:
            for iv in self.ivals.get(sp, []):
                if iv[0] < hi and lo < iv[1]:
                    iv[3].add(sid)

    def op(self, eng, fn, reads=(), writes=(), dma_slot=None, tag=""):
        sid = len(self.steps)
        semkey = ("d", dma_slot) if dma_slot is not None else ("e", eng)
        deps = self._deps_for(reads, writes)
        self.steps.append(dict(sid=sid, eng=eng, fn=fn, deps=deps, semkey=semkey,
                               dma=dma_slot is not None, needed=False, val=None,
                               tag=tag))
        self._update(sid, reads, writes)
        return sid

    def finalize(self, block):
        nc = self.nc
        steps = self.steps
        for s in steps:
            for dd in s["deps"]:
                steps[dd]["needed"] = True
        for s in steps:
            if s["dma"]:
                s["needed"] = True
        counts = {}
        for s in steps:
            if s["needed"]:
                inc = 16 if s["dma"] else 1
                v = counts.get(s["semkey"], 0) + inc
                counts[s["semkey"]] = v
                s["val"] = v
        semkeys = sorted(set(s["semkey"] for s in steps if s["needed"]))
        self._sem_ctxs = []
        self.sems = {}
        for i, sk in enumerate(semkeys):
            cm = nc.semaphore(f"s{i}")
            self.sems[sk] = cm.__enter__()
            self._sem_ctxs.append(cm)
        self.final_counts = counts
        per_eng = {e: [] for e in ("tensor", "vector", "scalar", "gpsimd", "sync")}
        for s in steps:
            per_eng[s["eng"]].append(s)

        eng_idx = {}
        cnt = {}
        for s in steps:
            if not s["dma"]:
                k = s["eng"]
                eng_idx[s["sid"]] = cnt.get(k, 0)
                cnt[k] = cnt.get(k, 0) + 1

        def mk_body(eng, mysteps):
            def body(e):
                waited = {}
                drain_cover = 0
                emitted = 0
                for s in mysteps:
                    want = {}
                    need_drain = False
                    for dd in s["deps"]:
                        ds = steps[dd]
                        if (not ds["dma"]) and ds["eng"] == eng:
                            if eng != "tensor" and eng_idx[dd] >= drain_cover:
                                need_drain = True
                            continue
                        sk, v = ds["semkey"], ds["val"]
                        if want.get(sk, 0) < v:
                            want[sk] = v
                    if need_drain:
                        e.drain()
                        drain_cover = emitted
                    for sk in sorted(want):
                        v = want[sk]
                        if waited.get(sk, 0) < v:
                            e.wait_ge(self.sems[sk], v)
                            waited[sk] = v
                    inst = s["fn"](e)
                    if s["needed"]:
                        inst.then_inc(self.sems[s["semkey"]], 16 if s["dma"] else 1)
                    if not s["dma"]:
                        emitted += 1
                if eng == "sync":
                    for sk in sorted(self.final_counts):
                        if sk[0] == "d":
                            v = self.final_counts[sk]
                            if waited.get(sk, 0) < v:
                                e.wait_ge(self.sems[sk], v)
            return body

        block.sync(mk_body("sync", per_eng["sync"]))
        block.tensor(mk_body("tensor", per_eng["tensor"]))
        block.vector(mk_body("vector", per_eng["vector"]))
        block.scalar(mk_body("scalar", per_eng["scalar"]))
        block.gpsimd(mk_body("gpsimd", per_eng["gpsimd"]))


def build_program(debug=(), stop_after=None):
    nc = bass.Bass("TRN2", target_bir_lowering=False)
    dbg = set(debug)

    def din(name, shape, dt):
        return nc.dram_tensor(name, shape, dt, kind="ExternalInput").ap()

    d = {}
    d["x"] = din("x", [S, HID], F32)
    d["x_own"] = din("x_own", [NSLOT * 128, HID], F32)
    d["posT_own"] = din("posT_own", [32, NSLOT * 128], F32)
    d["cmax"] = din("cmax", [128, TPC], F32)
    d["lmask"] = din("lmask", [128, TPC * 256], F32)
    for nm, sh in [("Wq", [HID, NH * D]), ("Wcmp", [HID, 1]), ("Wk", [HID, D]),
                   ("Wv", [HID, D]), ("Wlk", [HID, NH * D]), ("Wlv", [HID, NH * D]),
                   ("Wo", [NH * D, HID])]:
        d[nm] = din(nm, sh, F32)
    for nm, sh in [("bq", NH * D), ("bcmp", 1), ("bk", D), ("bv", D),
                   ("blk", NH * D), ("blv", NH * D), ("qn_w", D), ("kn_w", D),
                   ("bo", HID), ("s_gate", FFN), ("s_up", FFN), ("s_down", HID)]:
        d[nm] = din(nm, [sh], F32)
    d["wgT"] = din("wgT", [HID, FFN], I32)
    d["wuT"] = din("wuT", [HID, FFN], I32)
    d["wdT"] = din("wdT", [FFN, HID], I32)

    y = nc.dram_tensor("y", [TPC * 128, HID], F32, kind="ExternalOutput").ap()
    x2_hbm = nc.dram_tensor("x2_scratch", [TPC * 128, HID], F32).ap()

    dbg_outs = {}

    def dout(name, shape, dt=F32):
        dbg_outs[name] = nc.dram_tensor(name, shape, dt, kind="ExternalOutput").ap()
        return dbg_outs[name]

    with ExitStack() as ctx:
        mega_h = ctx.enter_context(nc.sbuf_tensor("mega", [128, MEGA_BYTES],
                                                  mybir.dt.int8))
        psum_h = ctx.enter_context(nc.psum_tensor("ps", [128, 4096], F32))
        small_h = ctx.enter_context(nc.sbuf_tensor("small", [128, 4096], F32))

        prog = Prog(nc)
        OP = prog.op

        mega_f = mega_h[:, :].bitcast(F32)
        mega_b = mega_h[:, :].bitcast(BF16)
        mega_i = mega_h[:, :].bitcast(I32)

        def MF(byte_lo, cols):
            assert byte_lo % 4 == 0 and byte_lo + 4 * cols <= MEGA_BYTES
            return mega_f[:, byte_lo // 4: byte_lo // 4 + cols], \
                ("mega", byte_lo, byte_lo + 4 * cols)

        def MB(byte_lo, cols):
            assert byte_lo % 2 == 0 and byte_lo + 2 * cols <= MEGA_BYTES
            return mega_b[:, byte_lo // 2: byte_lo // 2 + cols], \
                ("mega", byte_lo, byte_lo + 2 * cols)

        def MI(byte_lo, cols):
            return mega_i[:, byte_lo // 4: byte_lo // 4 + cols], \
                ("mega", byte_lo, byte_lo + 4 * cols)

        psum_f = psum_h[:, :]
        psum_b = psum_h[:, :].bitcast(BF16)

        def PF(col_lo, cols):
            return psum_f[:, col_lo:col_lo + cols], \
                ("psum", col_lo * 4, (col_lo + cols) * 4)

        def PB(bcol_lo, cols):
            return psum_b[:, bcol_lo:bcol_lo + cols], \
                ("psum", bcol_lo * 2, (bcol_lo + cols) * 2)

        small_off = [0]

        def SC(cols, dtype=F32):
            lo = small_off[0]
            n = cols if dtype == F32 else (cols + 1) // 2
            small_off[0] += n
            assert small_off[0] <= 4096
            if dtype == F32:
                return small_h[:, lo:lo + cols], ("small", lo * 4, (lo + cols) * 4)
            ap = small_h[:, :].bitcast(BF16)[:, 2 * lo:2 * lo + cols]
            return ap, ("small", lo * 4, lo * 4 + 2 * cols)

        def ROWF(byte_lo, cols):
            return mega_f[0:1, byte_lo // 4: byte_lo // 4 + cols], \
                ("mega", byte_lo, byte_lo + 4 * cols)

        def ROWB(byte_lo, cols):
            return mega_b[0:1, byte_lo // 2: byte_lo // 2 + cols], \
                ("mega", byte_lo, byte_lo + 2 * cols)

        rows_off = [18 * KB]

        def SR(cols, dtype=F32):
            lo = rows_off[0]
            nbytes = 4 * cols if dtype == F32 else 2 * cols
            rows_off[0] += (nbytes + 3) // 4 * 4
            assert rows_off[0] <= 24 * KB
            return (ROWF if dtype == F32 else ROWB)(lo, cols)

        def T32AT(byte_lo, cols):
            return mega_f[0:32, byte_lo // 4: byte_lo // 4 + cols], \
                ("mega", byte_lo, byte_lo + 4 * cols)

        ident_bf, B_identb = MB(0, 128)
        ident_f, B_identf = MF(256, 128)
        ones128, B_ones128 = MB(768, 128)
        pats, B_pats = MB(1 * KB, 4 * 128)
        wcmpr, B_wcmpr = MB(2 * KB, 2048)
        knrep, B_knrep = MF(6 * KB, 128)
        qnrep, B_qnrep = MF(6 * KB + 512, 128)
        tabO, B_tabO = MF(7 * KB, 2 * NSLOT * 32)
        tabB, B_tabB = MF(10 * KB, 2 * 16 * 32)
        lmask, B_lmask = MB(14 * KB, TPC * 256)
        ck_fm, B_ckfm = MB(24 * KB, 2048)
        cv_tm, B_cvtm = MB(28 * KB, 2048)
        ENT_LO = 32 * KB

        def ENT(b):
            return MB(ENT_LO + 4096 * b, 2048)

        posB, B_posB = T32AT(ENT_LO, 2048)
        posO, B_posO = T32AT(ENT_LO + 8 * KB, NSLOT * 128)
        angT, B_angT = T32AT(ENT_LO + 16 * KB, 2048)
        r_wcmp32, B_rw32 = ROWF(ENT_LO + 24 * KB, 2048)
        r_scr, B_rscr = ROWF(ENT_LO + 32 * KB, 2048)

        LNF_LO = 96 * KB

        def LNF(k, sl_lo, ncols):
            return MB(LNF_LO + (1536 * k + sl_lo) * 2, ncols)

        B_LNF_ALL = ("mega", LNF_LO, LNF_LO + 16 * 1536 * 2)
        lnst, B_lnst = MB(144 * KB, 2048)
        lnsto, B_lnsto = MB(148 * KB, 2048)
        wlsc, B_wlsc = MB(152 * KB, 2048)
        efmA, B_efmA = MB(156 * KB, 16 * 128)
        efmB, B_efmB = MB(160 * KB, 16 * 128)
        wkv, B_wkv = MB(164 * KB, 16 * 256)
        xst = [MF((172 + 8 * i) * KB, 2048) for i in range(2)]
        xo = [xst[0], xst[0]]

        bn24, B_bn24 = SC(24)
        bn2, B_bn2 = SC(2)
        lncols, B_lncols = SC(4)
        wcc, B_wcc = SC(8)
        cmax_sb, B_cmax = SC(TPC)
        bcmpc, B_bcmpc = SC(1)
        idsc, B_idsc = SC(128)
        ckt, B_ckt = SC(256)
        rmsc, B_rmsc = SC(2)
        ropet, B_ropet = SC(64)
        wdiagA, B_wdA = SC(128, BF16)
        wdiagB, B_wdB = SC(128, BF16)
        wrap0, B_wrap0 = SC(512)
        wrap1, B_wrap1 = SC(512)
        wrapi_lo = small_off[0]
        wrapi = small_h[:, :].bitcast(I32)[:, wrapi_lo:wrapi_lo + 512]
        B_wrapi = ("small", wrapi_lo * 4, (wrapi_lo + 512) * 4)
        small_off[0] += 512

        ones_row, B_ones = SR(128, BF16)
        r_wcmpb, B_rwb = ROWB(ENT_LO + 40 * KB, 2048)
        r_bkv, B_rbkv = SR(256)
        r_bkv_hi, B_rbkvh = SR(256, BF16)
        r_bkv_lo, B_rbkvl = SR(256, BF16)
        r_kn, B_rkn = SR(128)
        r_qn, B_rqn = SR(128)
        r_bcmp, B_rbc = SR(1)
        r_bcmpb, B_rbcb = SR(2, BF16)

        iv32 = small_h[0:32, 4000:4001]
        B_iv32 = ("small", 4000 * 4, 4001 * 4)

        def IOTA(eng_out, wb, pattern, base, cm):
            OP("gpsimd", lambda e: e.iota(eng_out, pattern=pattern, base=base,
                                          channel_multiplier=cm,
                                          allow_small_or_imprecise_dtypes=True),
               writes=[wb], tag="iota")

        IOTA(idsc, B_idsc, [[-1, 128]], 0, 1)
        OP("vector", lambda e: e.tensor_scalar(ident_bf, idsc, 0.0, None, ALU.is_equal),
           reads=[B_idsc], writes=[B_identb])
        OP("vector", lambda e: e.tensor_scalar(ident_f, idsc, 0.0, None, ALU.is_equal),
           reads=[B_idsc], writes=[B_identf])
        OP("vector", lambda e: e.memset(ones_row, 1.0), writes=[B_ones])
        OP("vector", lambda e: e.memset(ones128, 1.0), writes=[B_ones128])
        pio, B_pio = SC(128)
        pt1, B_pt1 = SC(128)
        pt2c, B_pt2c = SC(128)
        IOTA(pio, B_pio, [[-4, 128]], 0, 1)
        for j in range(4):
            pat = pats[:, 128 * j:128 * (j + 1)]
            OP("vector", lambda e, j=j: e.tensor_scalar(pt1, pio, float(-128 * j),
                                                        None, ALU.is_ge),
               reads=[B_pio], writes=[B_pt1], tag="pat")
            OP("vector", lambda e, j=j: e.tensor_scalar(pt2c, pio, float(-128 * j + 3),
                                                        None, ALU.is_le),
               reads=[B_pio], writes=[B_pt2c], tag="pat")
            OP("vector", lambda e, pat=pat: e.tensor_mul(pat, pt1, pt2c),
               reads=[B_pt1, B_pt2c], writes=[B_pats], tag="pat")

        def load_row(dst, B_dst, src_dram, n, slot="ld0"):
            OP("sync", lambda e: e.dma_start(out=dst, in_=src_dram[None, :]),
               writes=[B_dst], dma_slot=slot)

        OP("sync", lambda e: e.dma_start(out=r_wcmp32,
                                         in_=d["Wcmp"].rearrange("h one -> one h")),
           writes=[B_rw32], dma_slot="ld0")
        OP("vector", lambda e: e.tensor_copy(r_wcmpb, r_wcmp32),
           reads=[B_rw32], writes=[B_rwb])
        pW, B_pW = PF(0, 2048)
        for q in range(4):
            OP("tensor", lambda e, q=q: e.matmul(pW[:, 512 * q:512 * (q + 1)], ones_row,
                                                 r_wcmpb[:, 512 * q:512 * (q + 1)],
                                                 start=True, stop=True),
               reads=[B_ones, B_rwb], writes=[B_pW])
        OP("vector", lambda e: e.tensor_copy(wcmpr, pW), reads=[B_pW], writes=[B_wcmpr])

        load_row(r_bcmp, B_rbc, d["bcmp"], 1)
        OP("vector", lambda e: e.tensor_copy(r_bcmpb[:, 0:1], r_bcmp),
           reads=[B_rbc], writes=[B_rbcb])
        pB1, B_pB1 = PF(2048, 64)
        OP("tensor", lambda e: e.matmul(pB1[:, 0:1], ones_row, r_bcmpb[:, 0:1],
                                        start=True, stop=True),
           reads=[B_ones, B_rbcb], writes=[B_pB1])
        OP("vector", lambda e: e.tensor_copy(bcmpc, pB1[:, 0:1]),
           reads=[B_pB1], writes=[B_bcmpc])

        def replicate_f32(dst_ap, B_dst, src_row, B_src, cols, psum_col, scale=None):
            hi, B_hi = SR(cols, BF16)
            hi32, B_hi32 = SR(cols)
            lo, B_lo = SR(cols, BF16)
            OP("vector", lambda e: e.tensor_copy(hi, src_row), reads=[B_src], writes=[B_hi])
            OP("vector", lambda e: e.tensor_copy(hi32, hi), reads=[B_hi], writes=[B_hi32])
            OP("vector", lambda e: e.tensor_sub(hi32, src_row, hi32),
               reads=[B_src, B_hi32], writes=[B_hi32])
            OP("vector", lambda e: e.tensor_copy(lo, hi32), reads=[B_hi32], writes=[B_lo])
            pp, B_pp = PF(psum_col, cols)
            for q in range((cols + 511) // 512):
                sl = slice(512 * q, min(cols, 512 * (q + 1)))
                OP("tensor", lambda e, sl=sl: e.matmul(pp[:, sl], ones_row, hi[:, sl],
                                                       start=True, stop=False),
                   reads=[B_ones, B_hi], writes=[B_pp])
                OP("tensor", lambda e, sl=sl: e.matmul(pp[:, sl], ones_row, lo[:, sl],
                                                       start=False, stop=True),
                   reads=[B_ones, B_lo], writes=[B_pp])
            if scale is None:
                OP("vector", lambda e: e.tensor_copy(dst_ap, pp), reads=[B_pp], writes=[B_dst])
            else:
                OP("vector", lambda e: e.tensor_scalar_mul(dst_ap, pp, scale),
                   reads=[B_pp], writes=[B_dst])

        load_row(r_kn, B_rkn, d["kn_w"], D)
        load_row(r_qn, B_rqn, d["qn_w"], D)
        replicate_f32(knrep, B_knrep, r_kn, B_rkn, 128, 2112)
        replicate_f32(qnrep, B_qnrep, r_qn, B_rqn, 128, 2112, scale=SCALE)

        OP("sync", lambda e: e.dma_start(out=r_bkv[:, 0:128], in_=d["bk"][None, :]),
           writes=[B_rbkv], dma_slot="ld0")
        OP("sync", lambda e: e.dma_start(out=r_bkv[:, 128:256], in_=d["bv"][None, :]),
           writes=[B_rbkv], dma_slot="ld0")
        OP("vector", lambda e: e.tensor_copy(r_bkv_hi, r_bkv), reads=[B_rbkv], writes=[B_rbkvh])
        OP("vector", lambda e: e.tensor_copy(r_scr[:, 0:256], r_bkv_hi),
           reads=[B_rbkvh], writes=[B_rscr])
        OP("vector", lambda e: e.tensor_sub(r_scr[:, 0:256], r_bkv, r_scr[:, 0:256]),
           reads=[B_rbkv, B_rscr], writes=[B_rscr])
        OP("vector", lambda e: e.tensor_copy(r_bkv_lo, r_scr[:, 0:256]),
           reads=[B_rscr], writes=[B_rbkvl])

        OP("sync", lambda e: e.dma_start(out=cmax_sb, in_=d["cmax"][:, :]),
           writes=[B_cmax], dma_slot="ld0")
        sgup, B_sgup = SC(88)
        sdown, B_sdown = SC(16)
        OP("sync", lambda e: e.dma_start(out=sgup[:, 0:44],
                                         in_=d["s_gate"].rearrange("(m p) -> p m", p=128),
                                         allow_slow_non_contiguous=True),
           writes=[B_sgup], dma_slot="ld0")
        OP("sync", lambda e: e.dma_start(out=sgup[:, 44:88],
                                         in_=d["s_up"].rearrange("(m p) -> p m", p=128),
                                         allow_slow_non_contiguous=True),
           writes=[B_sgup], dma_slot="ld0")
        OP("sync", lambda e: e.dma_start(out=sdown,
                                         in_=d["s_down"].rearrange("(m p) -> p m", p=128),
                                         allow_slow_non_contiguous=True),
           writes=[B_sdown], dma_slot="ld0")
        OP("gpsimd", lambda e: e.dma_start(out=lmask, in_=d["lmask"][:, :]),
           writes=[B_lmask], dma_slot="ldg")

        IOTA(iv32, B_iv32, [[0, 1]], 0, 1)
        LN1E4_32 = float(np.log(10000.0) / 32.0)
        OP("scalar", lambda e: e.activation(iv32, iv32, AF.Exp, scale=-LN1E4_32),
           reads=[B_iv32], writes=[B_iv32])
        IOTA(posB, B_posB, [[4, 2048]], 3, 0)
        OP("sync", lambda e: e.dma_start(out=posO, in_=d["posT_own"][:, :]),
           writes=[B_posO], dma_slot="ld0")

        ptr, B_ptr = PF(2112, 512)

        def build_tables(pos_ap, B_pos, n, dst_ap, B_dst):
            OP("vector", lambda e: e.tensor_scalar(angT[:, 0:n * 128], pos_ap, iv32,
                                                   None, ALU.mult),
               reads=[B_pos, B_iv32], writes=[B_angT])
            done = 0
            while done < n:
                grp = min(16, n - done)
                g = 32 * grp
                for j in range(grp):
                    OP("tensor", lambda e, j=j, dn=done: e.transpose(
                        ptr[:, 32 * j:32 * (j + 1)],
                        angT[:, 128 * (dn + j):128 * (dn + j + 1)],
                        ident_f[0:32, 0:32]),
                       reads=[B_angT, B_identf], writes=[B_ptr], tag="ttab")
                co = dst_ap[:, 32 * done:32 * done + g]
                so = dst_ap[:, n * 32 + 32 * done:n * 32 + 32 * done + g]
                for shift, outa in ((0.0, so), (PI / 2, co)):
                    if shift == 0.0:
                        OP("vector", lambda e, g=g: e.tensor_scalar_mul(
                            wrap0[:, 0:g], ptr[:, 0:g], 1.0 / (2 * PI)),
                           reads=[B_ptr], writes=[B_wrap0])
                    else:
                        OP("vector", lambda e, g=g, sh=shift: e.tensor_scalar(
                            wrap0[:, 0:g], ptr[:, 0:g], sh, 1.0 / (2 * PI),
                            ALU.add, ALU.mult),
                           reads=[B_ptr], writes=[B_wrap0])
                    OP("vector", lambda e, g=g: e.tensor_copy(wrapi[:, 0:g], wrap0[:, 0:g]),
                       reads=[B_wrap0], writes=[B_wrapi])
                    OP("vector", lambda e, g=g: e.tensor_copy(wrap1[:, 0:g], wrapi[:, 0:g]),
                       reads=[B_wrapi], writes=[B_wrap1])
                    OP("vector", lambda e, g=g: e.tensor_sub(wrap0[:, 0:g], wrap0[:, 0:g],
                                                             wrap1[:, 0:g]),
                       reads=[B_wrap0, B_wrap1], writes=[B_wrap0])
                    OP("vector", lambda e, g=g: e.tensor_scalar_mul(wrap0[:, 0:g],
                                                                    wrap0[:, 0:g], 2 * PI),
                       reads=[B_wrap0], writes=[B_wrap0])
                    OP("vector", lambda e, g=g: e.tensor_scalar(
                        wrap1[:, 0:g], wrap0[:, 0:g], PI, -2 * PI, ALU.is_gt, ALU.mult),
                       reads=[B_wrap0], writes=[B_wrap1])
                    OP("vector", lambda e, g=g: e.tensor_add(wrap0[:, 0:g], wrap0[:, 0:g],
                                                             wrap1[:, 0:g]),
                       reads=[B_wrap0, B_wrap1], writes=[B_wrap0])
                    OP("vector", lambda e, g=g: e.tensor_scalar(
                        wrap1[:, 0:g], wrap0[:, 0:g], -PI, 2 * PI, ALU.is_lt, ALU.mult),
                       reads=[B_wrap0], writes=[B_wrap1])
                    OP("vector", lambda e, g=g: e.tensor_add(wrap0[:, 0:g], wrap0[:, 0:g],
                                                             wrap1[:, 0:g]),
                       reads=[B_wrap0, B_wrap1], writes=[B_wrap0])
                    OP("vector", lambda e, g=g: e.tensor_scalar_min(wrap0[:, 0:g],
                                                                    wrap0[:, 0:g], PI),
                       reads=[B_wrap0], writes=[B_wrap0])
                    OP("vector", lambda e, g=g: e.tensor_scalar_max(wrap0[:, 0:g],
                                                                    wrap0[:, 0:g], -PI),
                       reads=[B_wrap0], writes=[B_wrap0])
                    OP("scalar", lambda e, g=g, oa=outa: e.activation(oa, wrap0[:, 0:g],
                                                                      AF.Sin),
                       reads=[B_wrap0], writes=[B_dst])
                done += grp

        build_tables(posB, B_posB, 16, tabB, B_tabB)
        build_tables(posO, B_posO, NSLOT, tabO, B_tabO)

        shuf1 = [i ^ 1 for i in range(32)]
        shuf2 = [i ^ 2 for i in range(32)]
        ent_ps, B_entps = PF(0, 2048)
        if "wseq" in dbg:
            wseq, B_wseq = MF(ENT_LO + 48 * KB, 2048)
            eseq, B_eseq = MF(ENT_LO + 56 * KB, 2048)

        def ln_tile(x_ap, B_x, out_ap, B_out, tag="", dump=None):
            dump_ap, B_dump = (wlsc, B_wlsc) if dump is None else dump
            sm = bn2[:, 0:1]
            ss = bn2[:, 1:2]
            inv = lncols[:, 0:1]
            nmi = lncols[:, 1:2]
            tt = lncols[:, 2:3]
            OP("vector", lambda e: e.reduce_sum(sm, x_ap, axis=AX.X),
               reads=[B_x], writes=[B_bn2], tag=tag)
            OP("vector", lambda e: e.memset(ss, 0.0), writes=[B_bn2], tag=tag)
            OP("vector", lambda e, dump_ap=dump_ap: e.scalar_tensor_tensor(
                out=dump_ap, in0=x_ap, scalar=1.0, in1=x_ap,
                op0=ALU.bypass, op1=ALU.mult, accum_out=ss),
               reads=[B_x], writes=[B_dump, B_bn2], tag=tag)
            OP("vector", lambda e: e.tensor_scalar_mul(sm, sm, 1.0 / HID),
               reads=[B_bn2], writes=[B_bn2], tag=tag)
            OP("vector", lambda e: e.tensor_mul(tt, sm, sm),
               reads=[B_bn2], writes=[B_lncols], tag=tag)
            OP("vector", lambda e: e.tensor_scalar_mul(inv, ss, 1.0 / HID),
               reads=[B_bn2], writes=[B_lncols], tag=tag)
            OP("vector", lambda e: e.tensor_sub(inv, inv, tt),
               reads=[B_lncols], writes=[B_lncols], tag=tag)
            OP("vector", lambda e: e.tensor_scalar_add(inv, inv, EPS_LN),
               reads=[B_lncols], writes=[B_lncols], tag=tag)
            OP("scalar", lambda e: e.activation(inv, inv, AF.Sqrt),
               reads=[B_lncols], writes=[B_lncols], tag=tag)
            OP("vector", lambda e: e.reciprocal(inv, inv),
               reads=[B_lncols], writes=[B_lncols], tag=tag)
            OP("vector", lambda e: e.tensor_mul(nmi, sm, inv),
               reads=[B_bn2, B_lncols], writes=[B_lncols], tag=tag)
            OP("vector", lambda e: e.tensor_scalar_mul(nmi, nmi, -1.0),
               reads=[B_lncols], writes=[B_lncols], tag=tag)
            OP("scalar", lambda e: e.activation(out_ap, x_ap, AF.Identity, bias=nmi,
                                                scale=inv),
               reads=[B_x, B_lncols], writes=[B_out], tag=tag)

        for gt in range(NT):
            xb, B_xb = xst[gt % 2]
            OP("sync", lambda e, gt=gt, xb=xb: e.dma_start(
                out=xb, in_=d["x"][128 * gt:128 * (gt + 1), :]),
               writes=[B_xb], dma_slot=f"xs{gt % 2}")
            ln_tile(xb, B_xb, lnst, B_lnst, tag="p0ln")
            if "lncols" in dbg and gt == 63:
                OP("vector", lambda e: e.tensor_copy(cmax_sb[:, 0:2], bn2),
                   reads=[B_bn2], writes=[B_cmax])
                OP("vector", lambda e: e.tensor_copy(cmax_sb[:, 2:5], lncols[:, 0:3]),
                   reads=[B_lncols], writes=[B_cmax])
            wcol = wcc[:, 0:1]
            OP("vector", lambda e: e.memset(wcol, 0.0), writes=[B_wcc], tag="p0wz")
            OP("vector", lambda e: e.scalar_tensor_tensor(
                out=wlsc, in0=lnst, scalar=1.0, in1=wcmpr,
                op0=ALU.bypass, op1=ALU.mult, accum_out=wcol),
               reads=[B_lnst, B_wcmpr], writes=[B_wlsc, B_wcc], tag="p0wlog")
            OP("vector", lambda e: e.tensor_add(wcol, wcol, bcmpc),
               reads=[B_bcmpc], writes=[B_wcc])
            m4 = wcc[:, 1:2]; ee = wcc[:, 2:3]; s4 = wcc[:, 3:4]
            rr = wcc[:, 4:5]; t0 = wcc[:, 5:6]; nm4 = wcc[:, 6:7]
            OP("vector", lambda e: e.stream_shuffle(t0, wcol, shuf1), reads=[B_wcc], writes=[B_wcc])
            OP("vector", lambda e: e.tensor_max(m4, wcol, t0), reads=[B_wcc], writes=[B_wcc])
            OP("vector", lambda e: e.stream_shuffle(t0, m4, shuf2), reads=[B_wcc], writes=[B_wcc])
            OP("vector", lambda e: e.tensor_max(m4, m4, t0), reads=[B_wcc], writes=[B_wcc])
            OP("vector", lambda e: e.tensor_scalar_mul(nm4, m4, -1.0), reads=[B_wcc], writes=[B_wcc])
            OP("scalar", lambda e: e.activation(ee, wcol, AF.Exp, bias=nm4),
               reads=[B_wcc], writes=[B_wcc])
            OP("vector", lambda e: e.stream_shuffle(t0, ee, shuf1), reads=[B_wcc], writes=[B_wcc])
            OP("vector", lambda e: e.tensor_add(s4, ee, t0), reads=[B_wcc], writes=[B_wcc])
            OP("vector", lambda e: e.stream_shuffle(t0, s4, shuf2), reads=[B_wcc], writes=[B_wcc])
            OP("vector", lambda e: e.tensor_add(s4, s4, t0), reads=[B_wcc], writes=[B_wcc])
            OP("vector", lambda e: e.reciprocal(rr, s4), reads=[B_wcc], writes=[B_wcc])
            OP("vector", lambda e: e.tensor_mul(ee, ee, rr), reads=[B_wcc], writes=[B_wcc])
            if "wseq" in dbg and gt < 64:
                OP("vector", lambda e, gt=gt: e.tensor_copy(wseq[:, gt:gt + 1], wcol),
                   reads=[B_wcc], writes=[B_wseq])
                OP("vector", lambda e, gt=gt: e.tensor_copy(eseq[:, gt:gt + 1], ee),
                   reads=[B_wcc], writes=[B_eseq])
            j = gt % 4
            wd_ap, B_wd = (wdiagA, B_wdA) if gt % 2 == 0 else (wdiagB, B_wdB)
            OP("vector", lambda e, j=j, wd_ap=wd_ap: e.tensor_scalar(
                wd_ap, pats[:, 128 * j:128 * (j + 1)], ee, None, ALU.mult),
               reads=[B_pats, B_wcc], writes=[B_wd], tag="p0wd")
            for q in range(4):
                OP("tensor", lambda e, q=q, j=j, wd_ap=wd_ap: e.matmul(
                    ent_ps[:, 512 * q:512 * (q + 1)], wd_ap,
                    lnst[:, 512 * q:512 * (q + 1)], start=(j == 0), stop=(j == 3)),
                   reads=[B_wd, B_lnst], writes=[B_entps], tag="p0mm")
            if j == 3:
                b = gt // 4
                ent_ap, B_ent = ENT(b)
                OP("vector", lambda e, ent_ap=ent_ap: e.tensor_copy(ent_ap, ent_ps),
                   reads=[B_entps], writes=[B_ent], tag="p0ev")

        pt_bf, B_ptbf = PB(2 * 2048, 1024)
        for sl in range(NSLOT):
            xb, B_xb = xo[sl % 2]
            OP("sync", lambda e, sl=sl, xb=xb: e.dma_start(
                out=xb, in_=d["x_own"][128 * sl:128 * (sl + 1), :]),
               writes=[B_xb], dma_slot="xoQ")
            ln_tile(xb, B_xb, lnsto, B_lnsto, tag="p0own")
            for half in range(2):
                for k8 in range(8):
                    k = 8 * half + k8
                    OP("tensor", lambda e, k=k, k8=k8: e.transpose(
                        pt_bf[:, 128 * k8:128 * (k8 + 1)],
                        lnsto[:, 128 * k:128 * (k + 1)], ident_bf),
                       reads=[B_lnsto, B_identb], writes=[B_ptbf], tag="p0tr")
                for k8 in range(8):
                    k = 8 * half + k8
                    dst, B_dst = LNF(k, 128 * sl, 128)
                    OP("vector", lambda e, dst=dst, k8=k8: e.tensor_copy(
                        dst, pt_bf[:, 128 * k8:128 * (k8 + 1)]),
                       reads=[B_ptbf], writes=[B_dst], tag="p0trev")

        OP("gpsimd", lambda e: e.dma_start(
            out=wkv.rearrange("p (k c) -> p k c", k=16)[:, :, 0:128],
            in_=d["Wk"].rearrange("(k p) c -> p k c", p=128)),
           writes=[B_wkv], dma_slot="ldg")
        OP("gpsimd", lambda e: e.dma_start(
            out=wkv.rearrange("p (k c) -> p k c", k=16)[:, :, 128:256],
            in_=d["Wv"].rearrange("(k p) c -> p k c", p=128)),
           writes=[B_wkv], dma_slot="ldg")

        ck_ps, B_ckps = PF(2560, 256)
        for b in range(16):
            efm, B_efm = (efmA, B_efmA) if b % 2 == 0 else (efmB, B_efmB)
            ent_ap, B_ent = ENT(b)
            for half in range(2):
                for k8 in range(8):
                    k = 8 * half + k8
                    OP("tensor", lambda e, k=k, k8=k8, ent_ap=ent_ap: e.transpose(
                        pt_bf[:, 128 * k8:128 * (k8 + 1)],
                        ent_ap[:, 128 * k:128 * (k + 1)], ident_bf),
                       reads=[B_ent, B_identb], writes=[B_ptbf], tag="p1tr")
                for k8 in range(8):
                    k = 8 * half + k8
                    OP("vector", lambda e, k=k, k8=k8, efm=efm: e.tensor_copy(
                        efm[:, 128 * k:128 * (k + 1)], pt_bf[:, 128 * k8:128 * (k8 + 1)]),
                       reads=[B_ptbf], writes=[B_efm], tag="p1trev")
            OP("tensor", lambda e: e.matmul(ck_ps, ones_row, r_bkv_hi, start=True, stop=False),
               reads=[B_ones, B_rbkvh], writes=[B_ckps])
            OP("tensor", lambda e: e.matmul(ck_ps, ones_row, r_bkv_lo, start=False, stop=False),
               reads=[B_ones, B_rbkvl], writes=[B_ckps])
            wkv3 = wkv.rearrange("p (k c) -> p k c", k=16)
            for k in range(16):
                OP("tensor", lambda e, k=k, efm=efm: e.matmul(
                    ck_ps, efm[:, 128 * k:128 * (k + 1)], wkv3[:, k, :],
                    start=False, stop=(k == 15)),
                   reads=[B_efm, B_wkv], writes=[B_ckps], tag="p1mm")
            OP("vector", lambda e: e.tensor_copy(ckt, ck_ps), reads=[B_ckps], writes=[B_ckt])
            OP("vector", lambda e, b=b: e.tensor_copy(cv_tm[:, 128 * b:128 * (b + 1)],
                                                      ckt[:, 128:256]),
               reads=[B_ckt], writes=[B_cvtm])
            ss = rmsc[:, 0:1]
            OP("vector", lambda e: e.memset(ss, 0.0), writes=[B_rmsc])
            OP("scalar", lambda e: e.activation(lnsto[:, 0:128], ckt[:, 0:128], AF.Square,
                                                accum_out=ss),
               reads=[B_ckt], writes=[B_lnsto, B_rmsc], tag="p1rms")
            OP("vector", lambda e: e.tensor_scalar(ss, ss, 1.0 / 128, EPS_RMS,
                                                   ALU.mult, ALU.add),
               reads=[B_rmsc], writes=[B_rmsc])
            OP("scalar", lambda e: e.activation(ss, ss, AF.Sqrt), reads=[B_rmsc], writes=[B_rmsc])
            OP("vector", lambda e: e.reciprocal(ss, ss), reads=[B_rmsc], writes=[B_rmsc])
            OP("vector", lambda e: e.tensor_scalar(ckt[:, 0:128], ckt[:, 0:128], ss,
                                                   None, ALU.mult),
               reads=[B_rmsc], writes=[B_ckt])
            OP("vector", lambda e: e.tensor_mul(ckt[:, 0:128], ckt[:, 0:128], knrep),
               reads=[B_knrep], writes=[B_ckt])
            if "ckrope" in dbg and b == 15:
                OP("vector", lambda e: e.tensor_copy(wrap0[:, 0:128], ckt[:, 0:128]),
                   reads=[B_ckt], writes=[B_wrap0])
            cosb = tabB[:, 32 * b:32 * (b + 1)]
            sinb = tabB[:, 512 + 32 * b:512 + 32 * (b + 1)]
            r0 = ckt[:, 0:32]; r1 = ckt[:, 32:64]
            OP("vector", lambda e, cosb=cosb: e.tensor_mul(ropet[:, 0:32], r0, cosb),
               reads=[B_ckt, B_tabB], writes=[B_ropet])
            OP("vector", lambda e, sinb=sinb: e.tensor_mul(ropet[:, 32:64], r1, sinb),
               reads=[B_ckt, B_tabB], writes=[B_ropet])
            OP("vector", lambda e, cosb=cosb: e.tensor_mul(r1, r1, cosb),
               reads=[B_tabB], writes=[B_ckt])
            OP("vector", lambda e, sinb=sinb: e.tensor_mul(r0, r0, sinb),
               reads=[B_tabB], writes=[B_ckt])
            OP("vector", lambda e: e.tensor_add(ckt[:, 32:64], r0, r1),
               reads=[B_ckt], writes=[B_ckt])
            OP("vector", lambda e: e.tensor_sub(ckt[:, 0:32], ropet[:, 0:32],
                                                ropet[:, 32:64]),
               reads=[B_ropet], writes=[B_ckt])
            if "ckrope" in dbg and b == 15:
                OP("vector", lambda e: e.tensor_copy(wrap1[:, 0:128], ckt[:, 0:128]),
                   reads=[B_ckt], writes=[B_wrap1])
            OP("vector", lambda e: e.tensor_copy(lnsto[:, 0:128], ckt[:, 0:128]),
               reads=[B_ckt], writes=[B_lnsto])
            OP("tensor", lambda e: e.transpose(pt_bf[:, 0:128], lnsto[:, 0:128], ident_bf),
               reads=[B_lnsto, B_identb], writes=[B_ptbf])
            OP("vector", lambda e, b=b: e.tensor_copy(ck_fm[:, 128 * b:128 * (b + 1)],
                                                      pt_bf[:, 0:128]),
               reads=[B_ptbf], writes=[B_ckfm])


        WQT_LO = 32 * KB
        def WQT(k):
            return MB(WQT_LO + 1024 * k, 512)
        qtm, B_qtm = MF(48 * KB, 1024)
        qbf, B_qbf = MB(52 * KB, 2048)
        MSK_LO = 56 * KB
        msk_off = []
        acc = 0
        for t in range(TPC):
            msk_off.append(acc)
            acc += TILE_W[t] * 2
        assert MSK_LO + acc <= 76 * KB
        def MSK(t):
            return MB(MSK_LO + msk_off[t], TILE_W[t])
        Pbuf, B_Pbuf = MB(76 * KB, 2048)
        PT_LO = 80 * KB
        def PTv(par, cols):
            return MB(PT_LO + par * 4 * KB, cols)
        QG_LO = 88 * KB
        def QG(hh, t):
            return MB(QG_LO + (hh * 8 + t) * 256, 128)
        LKG_LO = 144 * KB
        def LKG(hh, sl):
            return MB(LKG_LO + (hh * 12 + sl) * 256, 128)
        def LKGW(hh, sl):
            return MB(LKG_LO + (hh * 12 + sl) * 256, 256)
        LVG_LO = 156 * KB
        def LVG(sl, hh):
            return MB(LVG_LO + (sl * 4 + hh) * 256, 128)
        MRG_LO = 168 * KB
        def MRG(hh):
            return MB(MRG_LO + hh * 2 * KB, 1024)
        brow_f, B_browf = ROWF(76 * KB, 512)
        brow_hi, B_browhi = ROWB(78 * KB, 512)
        brow_lo, B_browlo = ROWB(79 * KB, 512)

        tv8, B_tv8 = SC(8)
        st8, B_st8 = SC(8)
        rms4, B_rms4 = SC(4)
        slsb, B_slsb = SC(256)
        plsb, B_plsb = SC(256, BF16)
        pltsb, B_pltsb = SC(512, BF16)
        S_sb, B_Ssb = MF(48 * KB, 2048)

        ps_S = [PF(0, 512), PF(512, 512)]
        ps_mrg, B_psmrg = PF(1024, 1024)
        def ps_PT(par):
            return PB(2048 * 2 + par * 1024, 1024)
        ps_sl, B_pssl = PF(3072, 256)
        ps_PLT, B_psPLT = PB(3584 * 2, 512)

        iota_msk, B_iotam = MF(168 * KB, 2048)
        IOTA(iota_msk, B_iotam, [[1, 2048]], 0, 0)
        for t in range(TPC):
            m_ap, B_m = MSK(t)
            OP("vector", lambda e, m_ap=m_ap, t=t, W=TILE_W[t]: e.tensor_scalar(
                m_ap, iota_msk[:, 0:W], cmax_sb[:, t:t + 1], None, ALU.is_gt),
               reads=[B_iotam, B_cmax], writes=[B_m], tag="mskb")
            OP("vector", lambda e, m_ap=m_ap: e.tensor_scalar_mul(m_ap, m_ap, NEG),
               reads=[B_m], writes=[B_m], tag="mskb")

        r_scr2, B_rscr2 = ROWF(54 * KB, 512)

        def proj_load_bias(dsrc, g):
            OP("sync", lambda e: e.dma_start(out=brow_f, in_=dsrc[None, 512 * g:512 * (g + 1)]),
               writes=[B_browf], dma_slot="ld0")
            OP("vector", lambda e: e.tensor_copy(brow_hi, brow_f),
               reads=[B_browf], writes=[B_browhi])
            OP("vector", lambda e: e.tensor_copy(r_scr2[:, 0:512], brow_hi),
               reads=[B_browhi], writes=[B_rscr2])
            OP("vector", lambda e: e.tensor_sub(r_scr2[:, 0:512], brow_f, r_scr2[:, 0:512]),
               reads=[B_browf, B_rscr2], writes=[B_rscr2])
            OP("vector", lambda e: e.tensor_copy(brow_lo, r_scr2[:, 0:512]),
               reads=[B_rscr2], writes=[B_browlo])

        


        merged_hbm = nc.dram_tensor(
            "merged_scratch", [NH, 128, 1024], BF16,
            kind="ExternalOutput" if "merged" in dbg else "Internal").ap()
        B_mhbm = lambda h: ("dram_mrg", h * 1024, (h + 1) * 1024)

        ps_proj = [PF(2048, 512), PF(2560, 512)]
        pt2_bf = [PB(3072 * 2, 1024), PB(3584 * 2, 1024)]

        wq_all, B_wq_all = MB(WQT_LO, 16 * 512)

        def proj_dma_W(dsrc, g, slot):
            OP("gpsimd", lambda e, g=g, dsrc=dsrc: e.dma_start(
                out=wq_all.rearrange("p (k c) -> p k c", k=16),
                in_=dsrc[:, 512 * g:512 * (g + 1)].rearrange("(k p) c -> p k c", p=128)),
               writes=[B_wq_all], dma_slot=slot, tag="wdma")

        def rms_rope_store(sl, B_src_q, q_lo, hh_base, is_q, g, dst_fn, ntok_slot):
            qv = qtm[:, q_lo:q_lo + 512]
            v3 = qv.rearrange("p (h d) -> p h d", h=4)
            OP("vector", lambda e, v3=v3: e.memset(rms4, 0.0), writes=[B_rms4])
            OP("vector", lambda e, qv=qv: e.scalar_tensor_tensor(
                out=qbf[:, 512:1024], in0=qv, scalar=1.0, in1=qv,
                op0=ALU.bypass, op1=ALU.mult), reads=[B_qtm], writes=[B_qbf], tag="rr")
            OP("vector", lambda e: e.reduce_sum(
                rms4, qbf[:, 512:1024].rearrange("p (h d) -> p h d", h=4), axis=AX.X),
               reads=[B_qbf], writes=[B_rms4], tag="rr")
            OP("vector", lambda e: e.tensor_scalar_mul(rms4, rms4, 1.0 / 128),
               reads=[B_rms4], writes=[B_rms4])
            OP("vector", lambda e: e.tensor_scalar_add(rms4, rms4, EPS_RMS),
               reads=[B_rms4], writes=[B_rms4])
            OP("scalar", lambda e: e.activation(rms4, rms4, AF.Sqrt),
               reads=[B_rms4], writes=[B_rms4])
            OP("vector", lambda e: e.reciprocal(rms4, rms4),
               reads=[B_rms4], writes=[B_rms4])
            for hh in range(4):
                OP("vector", lambda e, v3=v3, hh=hh: e.tensor_scalar(
                    v3[:, hh, :], v3[:, hh, :], rms4[:, hh:hh + 1], None, ALU.mult),
                   reads=[B_rms4, B_qtm], writes=[B_qtm], tag="rr")
            gain = qnrep if is_q else knrep
            B_gain = B_qnrep if is_q else B_knrep
            OP("vector", lambda e, v3=v3, gain=gain: e.tensor_mul(
                v3, v3, gain.unsqueeze(1).to_broadcast([128, 4, 128])),
               reads=[B_qtm, B_gain], writes=[B_qtm], tag="rr")
            cosA = tabO[:, 32 * sl:32 * (sl + 1)]
            sinA = tabO[:, NSLOT * 32 + 32 * sl:NSLOT * 32 + 32 * (sl + 1)]
            cb = lambda t_: t_.unsqueeze(1).to_broadcast([128, 4, 32])
            r0 = v3[:, :, 0:32]; r1 = v3[:, :, 32:64]
            rt0 = qbf[:, 512:1024].bitcast(F32) if False else None
            t0v = slsb[:, 0:128].rearrange("p (h d) -> p h d", h=4)
            t1v = slsb[:, 128:256].rearrange("p (h d) -> p h d", h=4)
            OP("vector", lambda e, r0=r0, cosA=cosA, t0v=t0v: e.tensor_mul(
                t0v, r0, cb(cosA)), reads=[B_qtm, B_tabO], writes=[B_slsb], tag="rr")
            OP("vector", lambda e, r1=r1, sinA=sinA, t1v=t1v: e.tensor_mul(
                t1v, r1, cb(sinA)), reads=[B_qtm, B_tabO], writes=[B_slsb], tag="rr")
            OP("vector", lambda e, r1=r1, cosA=cosA: e.tensor_mul(r1, r1, cb(cosA)),
               reads=[B_tabO, B_qtm], writes=[B_qtm], tag="rr")
            OP("vector", lambda e, r0=r0, sinA=sinA: e.tensor_mul(r0, r0, cb(sinA)),
               reads=[B_tabO, B_qtm], writes=[B_qtm], tag="rr")
            OP("vector", lambda e, v3=v3, r0=r0, r1=r1: e.tensor_add(
                v3[:, :, 32:64], r0, r1), reads=[B_qtm], writes=[B_qtm], tag="rr")
            OP("vector", lambda e, v3=v3, t0v=t0v, t1v=t1v: e.tensor_sub(
                v3[:, :, 0:32], t0v, t1v), reads=[B_slsb, B_qtm], writes=[B_qtm], tag="rr")
            OP("vector", lambda e, qv=qv: e.tensor_copy(qbf[:, 0:512], qv),
               reads=[B_qtm], writes=[B_qbf], tag="rr")
            par = sl % 2
            pt_ap, B_pt = pt2_bf[par]
            for hh in range(4):
                OP("tensor", lambda e, hh=hh, pt_ap=pt_ap: e.transpose(
                    pt_ap[:, 128 * hh:128 * (hh + 1)],
                    qbf[:, 128 * hh:128 * (hh + 1)], ident_bf),
                   reads=[B_qbf, B_identb], writes=[B_pt], tag="rrT")
            for hh in range(4):
                dap, B_d = dst_fn(hh)
                OP("vector", lambda e, dap=dap, hh=hh, pt_ap=pt_ap: e.tensor_copy(
                    dap, pt_ap[:, 128 * hh:128 * (hh + 1)]),
                   reads=[B_pt], writes=[B_d], tag="rrTe")

        def lv_store(sl, q_lo, g):
            qv = qtm[:, q_lo:q_lo + 512]
            for hh in range(4):
                dap, B_d = LVG(sl, hh)
                OP("vector", lambda e, dap=dap, qv=qv, hh=hh: e.tensor_copy(
                    dap, qv[:, 128 * hh:128 * (hh + 1)]),
                   reads=[B_qtm], writes=[B_d], tag="lve")

        def proj_group(dW, dB, g, slots, kind, dma_slot):
            proj_dma_W(dW, g, dma_slot)
            proj_load_bias(dB, g)
            for i, sl in enumerate(slots):
                par = i % 2
                pp, B_pp = ps_proj[par]
                OP("tensor", lambda e, pp=pp: e.matmul(pp, ones_row, brow_hi,
                                                       start=True, stop=False),
                   reads=[B_ones, B_browhi], writes=[B_pp], tag="pj")
                OP("tensor", lambda e, pp=pp: e.matmul(pp, ones_row, brow_lo,
                                                       start=False, stop=False),
                   reads=[B_ones, B_browlo], writes=[B_pp], tag="pj")
                for k in range(16):
                    wap, B_w = WQT(k)
                    lap, B_l = LNF(k, 128 * sl, 128)
                    OP("tensor", lambda e, pp=pp, lap=lap, wap=wap, k=k: e.matmul(
                        pp, lap, wap, start=False, stop=(k == 15)),
                       reads=[B_l, B_w], writes=[B_pp], tag="pjmm")
                q_lo = 512 * par
                OP("vector", lambda e, pp=pp, q_lo=q_lo: e.tensor_copy(
                    qtm[:, q_lo:q_lo + 512], pp), reads=[B_pp], writes=[B_qtm], tag="pjev")
                if kind == "q":
                    t_own = i
                    rms_rope_store(sl, None, q_lo, 0, True, g,
                                   lambda hh, t_own=t_own: QG(hh, t_own), 1)
                elif kind == "lk":
                    rms_rope_store(sl, None, q_lo, 0, False, g,
                                   lambda hh, sl=sl: LKG(hh, sl), 1)
                else:
                    lv_store(sl, q_lo, g)

        def attn_head(g, hh):
            h = 4 * g + hh
            for pp_i in range(NPAIR):
                W = PAIR_W[pp_i]
                for ti in range(2):
                    t = 2 * pp_i + ti
                    sl_cur = 3 * pp_i + 1 + ti
                    nch = W // 512
                    qg_ap, B_qg = QG(hh, t)
                    for c in range(nch):
                        sp, B_sp = ps_S[c % 2]
                        OP("tensor", lambda e, sp=sp, qg_ap=qg_ap, c=c: e.matmul(
                            sp, qg_ap, ck_fm[:, 512 * c:512 * (c + 1)],
                            start=True, stop=True),
                           reads=[B_qg, B_ckfm], writes=[B_sp], tag="sc")
                        m_ap, B_m = MSK(t)
                        OP("vector", lambda e, sp=sp, m_ap=m_ap, c=c: e.tensor_add(
                            S_sb[:, 512 * c:512 * (c + 1)], sp, m_ap[:, 512 * c:512 * (c + 1)]),
                           reads=[B_sp, B_m], writes=[B_Ssb], tag="scev")
                    OP("vector", lambda e, W=W: e.max(out=tv8, in_=S_sb[:, 0:W]),
                       reads=[B_Ssb], writes=[B_tv8], tag="max8")
                    negm = st8[:, 0:1]; t2 = st8[:, 1:2]; Zc = st8[:, 2:3]
                    lnZ = st8[:, 3:4]; bia = st8[:, 4:5]
                    OP("vector", lambda e: e.tensor_scalar_mul(negm, tv8[:, 0:1], -1.0),
                       reads=[B_tv8], writes=[B_st8])
                    OP("vector", lambda e: e.tensor_scalar_max(t2, tv8[:, 7:8], -5.0e29),
                       reads=[B_tv8], writes=[B_st8])
                    OP("vector", lambda e: e.memset(Zc, 0.0), writes=[B_st8])
                    OP("scalar", lambda e: e.activation(qbf[:, 1024:1032], tv8, AF.Exp,
                                                        bias=negm, accum_out=Zc),
                       reads=[B_tv8, B_st8], writes=[B_qbf, B_st8], tag="exp8")
                    rZ = st8[:, 3:4]
                    OP("vector", lambda e: e.reciprocal(rZ, Zc),
                       reads=[B_st8], writes=[B_st8])
                    OP("scalar", lambda e, W=W: e.activation(
                        Pbuf[:, 0:W], S_sb[:, 0:W], AF.Exp, bias=negm),
                       reads=[B_Ssb, B_st8], writes=[B_Pbuf], tag="expP")
                    OP("vector", lambda e, W=W: e.scalar_tensor_tensor(
                        out=Pbuf[:, 0:W], in0=S_sb[:, 0:W], scalar=t2,
                        in1=Pbuf[:, 0:W], op0=ALU.is_ge, op1=ALU.mult),
                       reads=[B_Ssb, B_st8, B_Pbuf], writes=[B_Pbuf], tag="maskP")
                    OP("vector", lambda e, W=W: e.tensor_scalar(
                        Pbuf[:, 0:W], Pbuf[:, 0:W], rZ, None, ALU.mult),
                       reads=[B_st8, B_Pbuf], writes=[B_Pbuf], tag="scaleP")
                    if "snap" in dbg and h == 0 and t == 0:
                        OP("vector", lambda e: e.tensor_copy(snapB[:, 0:512], S_sb[:, 0:512]),
                           reads=[B_Ssb], writes=[B_snapB])
                        OP("vector", lambda e: e.tensor_copy(snapB[:, 512:520], tv8),
                           reads=[B_tv8], writes=[B_snapB])
                        OP("vector", lambda e: e.tensor_copy(snapB[:, 520:528], st8),
                           reads=[B_st8], writes=[B_snapB])
                        OP("vector", lambda e: e.tensor_copy(snapB[:, 528:1040], Pbuf[:, 0:512]),
                           reads=[B_Pbuf], writes=[B_snapB])
                    nkc = W // 128
                    pt_dst, B_ptd = PTv(ti, W)
                    done = 0
                    while done < nkc:
                        grp = min(8, nkc - done)
                        pta, B_pta = ps_PT((done // 8) % 2)
                        for j in range(grp):
                            OP("tensor", lambda e, j=j, dn=done, pta=pta: e.transpose(
                                pta[:, 128 * j:128 * (j + 1)],
                                Pbuf[:, 128 * (dn + j):128 * (dn + j + 1)], ident_bf),
                               reads=[B_Pbuf, B_identb], writes=[B_pta], tag="PT")
                        OP("vector", lambda e, dn=done, grp=grp, pta=pta, pt_dst=pt_dst: e.tensor_copy(
                            pt_dst[:, 128 * dn:128 * (dn + grp)], pta[:, 0:128 * grp]),
                           reads=[B_pta], writes=[B_ptd], tag="PTev")
                        done += grp
                    lkw, B_lkw = LKGW(hh, sl_cur - 1)
                    OP("tensor", lambda e, qg_ap=qg_ap, lkw=lkw: e.matmul(
                        ps_sl, qg_ap, lkw, start=True, stop=True),
                       reads=[B_qg, B_lkw], writes=[B_pssl], tag="sl")
                    OP("vector", lambda e, t=t: e.tensor_add(
                        slsb, ps_sl, lmask[:, 256 * t:256 * (t + 1)]),
                       reads=[B_pssl, B_lmask], writes=[B_slsb], tag="slev")
                    ml = st8[:, 5:6]; Zl = st8[:, 6:7]; rl = st8[:, 7:8]
                    OP("vector", lambda e: e.reduce_max(ml, slsb, axis=AX.X),
                       reads=[B_slsb], writes=[B_st8])
                    OP("vector", lambda e: e.tensor_scalar_mul(ml, ml, -1.0),
                       reads=[B_st8], writes=[B_st8])
                    OP("vector", lambda e: e.memset(Zl, 0.0), writes=[B_st8])
                    OP("scalar", lambda e: e.activation(plsb, slsb, AF.Exp, bias=ml,
                                                        accum_out=Zl),
                       reads=[B_slsb, B_st8], writes=[B_plsb, B_st8], tag="expL")
                    OP("vector", lambda e: e.reciprocal(rl, Zl),
                       reads=[B_st8], writes=[B_st8])
                    OP("vector", lambda e: e.tensor_scalar(plsb, plsb, rl, None, ALU.mult),
                       reads=[B_st8, B_plsb], writes=[B_plsb], tag="plm")
                    if "snap" in dbg and h == 0 and t == 0:
                        OP("vector", lambda e: e.tensor_copy(snapB[:, 1040:1296], slsb),
                           reads=[B_slsb], writes=[B_snapB])
                        OP("vector", lambda e: e.tensor_copy(snapA[:, 3072:3328], plsb),
                           reads=[B_plsb], writes=[B_snapA])
                    for j in range(2):
                        OP("tensor", lambda e, j=j: e.transpose(
                            ps_PLT[:, 128 * j:128 * (j + 1)],
                            plsb[:, 128 * j:128 * (j + 1)], ident_bf),
                           reads=[B_plsb, B_identb], writes=[B_psPLT], tag="PLT")
                    OP("vector", lambda e, ti=ti: e.tensor_copy(
                        pltsb[:, 256 * ti:256 * (ti + 1)], ps_PLT[:, 0:256]),
                       reads=[B_psPLT], writes=[B_pltsb], tag="PLTev")
                nkc = W // 128
                for kc in range(nkc):
                    for ti in range(2):
                        t = 2 * pp_i + ti
                        bank_first = (pp_i % 2 == 0) and (kc == 0) and (ti == 0)
                        pt_dst, B_ptd = PTv(ti, W)
                        OP("tensor", lambda e, kc=kc, t=t, pt_dst=pt_dst,
                           st=bank_first: e.matmul(
                            ps_mrg[:, 128 * t:128 * (t + 1)],
                            cv_tm[:, 128 * kc:128 * (kc + 1)],
                            pt_dst[:, 128 * kc:128 * (kc + 1)],
                            start=st, stop=False, skip_group_check=True),
                           reads=[B_cvtm, B_ptd], writes=[B_psmrg], tag="ctx")
                for ti in range(2):
                    t = 2 * pp_i + ti
                    sl_cur = 3 * pp_i + 1 + ti
                    for j, sl_kv in enumerate((sl_cur - 1, sl_cur)):
                        bank_last = (pp_i % 2 == 1) and (ti == 1) and (j == 1)
                        lv_ap, B_lv = LVG(sl_kv, hh)
                        OP("tensor", lambda e, lv_ap=lv_ap, j=j, t=t, ti=ti,
                           sp=bank_last: e.matmul(
                            ps_mrg[:, 128 * t:128 * (t + 1)], lv_ap,
                            pltsb[:, 256 * ti + 128 * j:256 * ti + 128 * (j + 1)],
                            start=False, stop=sp, skip_group_check=True),
                           reads=[B_lv, B_pltsb], writes=[B_psmrg], tag="lctx")
            mg_ap, B_mg = MRG(hh)
            OP("vector", lambda e, mg_ap=mg_ap: e.tensor_scalar_mul(mg_ap, ps_mrg, 0.5),
               reads=[B_psmrg], writes=[B_mg], tag="mgev")
            OP("sync", lambda e, h=h, mg_ap=mg_ap: e.dma_start(
                out=merged_hbm[h, :, :], in_=mg_ap),
               reads=[B_mg], writes=[B_mhbm(h)], dma_slot="mrg")

        snapA, B_snapA = MB(176 * KB, 4096)
        snapB, B_snapB = MF(184 * KB, 1536)
        for g in range(NHG):
            own_slots_q = [3 * (t // 2) + 1 + (t % 2) for t in range(TPC)]
            proj_group(d["Wq"], d["bq"], g, own_slots_q, "q", "wq")
            proj_group(d["Wlk"], d["blk"], g, list(range(NSLOT)), "lk", "wq")
            proj_group(d["Wlv"], d["blv"], g, list(range(NSLOT)), "lv", "wq")
            for hh in range(4):
                attn_head(g, hh)


        WO_LO = 96 * KB

        def WOc(k):
            return MB(WO_LO + 4 * KB * k, 2048)

        B_WO_ALL = ("mega", WO_LO, WO_LO + 64 * KB)
        wo_all = mega_b[:, WO_LO // 2: WO_LO // 2 + 16 * 2048]
        OP("gpsimd", lambda e: e.dma_start(
            out=wo_all.rearrange("p (k c) -> p k c", k=16),
            in_=d["Wo"].rearrange("(k p) c -> p k c", p=128)),
           writes=[B_WO_ALL], dma_slot="wo", tag="wodma")
        bo_f, B_bof = ROWF(76 * KB, 2048)
        bo_hi, B_bohi = ROWB(84 * KB, 2048)
        bo_lo, B_bolo = ROWB(48 * KB, 2048)
        OP("sync", lambda e: e.dma_start(out=bo_f, in_=d["bo"][None, :]),
           writes=[B_bof], dma_slot="ld0")

        borep, B_borep = MF(44 * KB, 2048)
        OP("vector", lambda e: e.tensor_copy(bo_hi, bo_f), reads=[B_bof], writes=[B_bohi])
        OP("vector", lambda e: e.tensor_sub(bo_f, bo_f, bo_hi),
           reads=[B_bohi], writes=[B_bof])
        OP("vector", lambda e: e.tensor_copy(bo_lo, bo_f), reads=[B_bof], writes=[B_bolo])
        pbr, B_pbr = PF(0, 2048)
        for q in range(4):
            OP("tensor", lambda e, q=q: e.matmul(pbr[:, 512 * q:512 * (q + 1)], ones_row,
                                                 bo_hi[:, 512 * q:512 * (q + 1)],
                                                 start=True, stop=False),
               reads=[B_ones, B_bohi], writes=[B_pbr])
            OP("tensor", lambda e, q=q: e.matmul(pbr[:, 512 * q:512 * (q + 1)], ones_row,
                                                 bo_lo[:, 512 * q:512 * (q + 1)],
                                                 start=False, stop=True),
               reads=[B_ones, B_bolo], writes=[B_pbr])
        OP("vector", lambda e: e.tensor_copy(borep, pbr), reads=[B_pbr], writes=[B_borep])
        mstg = [MB(88 * KB, 2048), MB(92 * KB, 2048)]
        xin3, B_xin3 = MF(172 * KB, 2048)
        x2t, B_x2t = MF(180 * KB, 2048)
        LN2_OFF = [56 + 2 * k for k in range(10)] + [2 + 2 * (k - 10) for k in range(10, 16)]

        def LN2c(k):
            return MB(LN2_OFF[k] * KB, 1024)

        ln2st, B_ln2st = MB(36 * KB, 2048)
        ps_wo, B_pswo = PF(0, 2048)
        for t in range(TPC):
            ms_ap, B_ms = mstg[t % 2]
            OP("sync", lambda e, t=t, ms_ap=ms_ap: e.dma_start(
                out=ms_ap.rearrange("p (h c) -> p h c", h=16),
                in_=merged_hbm[:, :, 128 * t:128 * (t + 1)].transpose([1, 0, 2])),
               reads=[("dram_mrg", 0, NH * 1024)], writes=[B_ms], dma_slot=f"ms{t % 2}",
               tag="msdma")
            sl_x = 3 * (t // 2) + 1 + (t % 2)
            OP("sync", lambda e, sl_x=sl_x: e.dma_start(
                out=xin3, in_=d["x_own"][128 * sl_x:128 * (sl_x + 1), :]),
               writes=[B_xin3], dma_slot="xo3", tag="x3dma")
            for h in range(NH):
                wo_ap, B_woc = WOc(h)
                for b in range(4):
                    OP("tensor", lambda e, h=h, b=b, ms_ap=ms_ap, wo_ap=wo_ap: e.matmul(
                        ps_wo[:, 512 * b:512 * (b + 1)],
                        ms_ap[:, 128 * h:128 * (h + 1)],
                        wo_ap[:, 512 * b:512 * (b + 1)],
                        start=(h == 0), stop=(h == NH - 1)),
                       reads=[B_ms, B_woc], writes=[B_pswo], tag="womm")
            if "p3s" in dbg and t == 1:
                cap, B_cap = MB(40 * KB, 2048)
                OP("vector", lambda e, cap=cap, ms_ap=ms_ap: e.tensor_copy(cap, ms_ap),
                   reads=[B_ms], writes=[B_cap], tag="p3cap")
                cap2, B_cap2 = MF(24 * KB, 2048)
                OP("vector", lambda e, cap2=cap2: e.tensor_copy(cap2, ps_wo),
                   reads=[B_pswo], writes=[B_cap2], tag="p3cap")
            OP("vector", lambda e: e.tensor_add(x2t, ps_wo, xin3),
               reads=[B_pswo, B_xin3], writes=[B_x2t], tag="x2ev")
            OP("vector", lambda e: e.tensor_add(x2t, x2t, borep),
               reads=[B_borep], writes=[B_x2t], tag="x2ev")
            OP("sync", lambda e, t=t: e.dma_start(
                out=y[128 * t:128 * (t + 1), :], in_=x2t),
               reads=[B_x2t], writes=[("dram_y", 0, 1)], dma_slot="yw", tag="ywr")
            ln_tile(x2t, B_x2t, ln2st, B_ln2st, tag="ln2", dump=MB(32 * KB, 2048))
            for half in range(2):
                for k8 in range(8):
                    k = 8 * half + k8
                    OP("tensor", lambda e, k=k, k8=k8: e.transpose(
                        pt_bf[:, 128 * k8:128 * (k8 + 1)],
                        ln2st[:, 128 * k:128 * (k + 1)], ident_bf),
                       reads=[B_ln2st, B_identb], writes=[B_ptbf], tag="ln2tr")
                for k8 in range(8):
                    k = 8 * half + k8
                    dap, B_d2 = LN2c(k)
                    OP("vector", lambda e, dap=dap, k8=k8, t=t: e.tensor_copy(
                        dap[:, 128 * t:128 * (t + 1)], pt_bf[:, 128 * k8:128 * (k8 + 1)]),
                       reads=[B_ptbf],
                       writes=[("mega", LN2_OFF[k] * KB + 256 * t,
                                LN2_OFF[k] * KB + 256 * (t + 1))],
                       tag="ln2te")

        run_p4 = stop_after != "p3"
        cs_f, B_csf = ROWF(76 * KB, 1024)
        cs_hi, B_cshi = ROWB(80 * KB, 1024)
        cs_lo, B_cslo = ROWB(82 * KB, 1024)
        ps_cs, B_pscs = PF(2048, 1024)

        def colsum_rows(src_chunks, nk, tag):
            for k in range(nk):
                cap, B_c = src_chunks(k)
                for bb in range(2):
                    OP("tensor", lambda e, cap=cap, bb=bb, k=k: e.matmul(
                        ps_cs[0:1, 512 * bb:512 * (bb + 1)], ones128[:, 0:1],
                        cap[:, 512 * bb:512 * (bb + 1)],
                        start=(k == 0), stop=(k == nk - 1)),
                       reads=[B_c, B_ones128], writes=[B_pscs], tag=tag)
            OP("vector", lambda e: e.tensor_scalar_mul(cs_f, ps_cs[0:1, :], -128.0),
               reads=[B_pscs], writes=[B_csf], tag=tag)
            OP("vector", lambda e: e.tensor_copy(cs_hi, cs_f), reads=[B_csf],
               writes=[B_cshi], tag=tag)
            OP("vector", lambda e: e.tensor_copy(cs_f2, cs_hi), reads=[B_cshi],
               writes=[B_csf2], tag=tag)
            OP("vector", lambda e: e.tensor_sub(cs_f2, cs_f, cs_f2),
               reads=[B_csf, B_csf2], writes=[B_csf2], tag=tag)
            OP("vector", lambda e: e.tensor_copy(cs_lo, cs_f2), reads=[B_csf2],
               writes=[B_cslo], tag=tag)

        cs_f2, B_csf2 = ROWF(84 * KB, 1024)
        if run_p4:
            colsum_rows(LN2c, 16, "cs1")

        wgbuf = [MB(88 * KB, 2048), MB(92 * KB, 2048)]
        silu_g, B_silug = MB(24 * KB, 1024)
        gu_off = (list(range(96, 172, 2)) + [26, 28, 30] + [172, 174, 176])[:44]

        def GU(m):
            return MB(gu_off[m] * KB, 1024)

        ps_g, B_psg = PF(0, 1024)
        ps_u, B_psu = PF(1024, 1024)
        for m in range(44 if run_p4 else 0):
            wg_ap, B_wg = wgbuf[m % 2]
            for which, dW4 in (("g", d["wgT"]), ("u", d["wuT"])):
                OP("gpsimd", lambda e, m=m, dW4=dW4, wg_ap=wg_ap: e.dma_start(
                    out=wg_ap.rearrange("p (k c) -> p k c", k=16),
                    in_=dW4[:, 128 * m:128 * (m + 1)].rearrange("(k p) c -> p k c", p=128)),
                   writes=[B_wg], dma_slot=f"w{which}{m % 2}", tag="ffnwdma")
                pp, B_pp = (ps_g, B_psg) if which == "g" else (ps_u, B_psu)
                for bb in range(2):
                    OP("tensor", lambda e, bb=bb, pp=pp: e.matmul(
                        pp[:, 512 * bb:512 * (bb + 1)], ones_row,
                        cs_hi[:, 512 * bb:512 * (bb + 1)], start=True, stop=False),
                       reads=[B_ones, B_cshi], writes=[B_pp], tag="gseed")
                    OP("tensor", lambda e, bb=bb, pp=pp: e.matmul(
                        pp[:, 512 * bb:512 * (bb + 1)], ones_row,
                        cs_lo[:, 512 * bb:512 * (bb + 1)], start=False, stop=False),
                       reads=[B_ones, B_cslo], writes=[B_pp], tag="gseed")
                for k in range(16):
                    lap, B_l2 = LN2c(k)
                    for bb in range(2):
                        OP("tensor", lambda e, k=k, bb=bb, wg_ap=wg_ap, lap=lap,
                           pp=pp: e.matmul(
                            pp[:, 512 * bb:512 * (bb + 1)],
                            wg_ap[:, 128 * k:128 * (k + 1)],
                            lap[:, 512 * bb:512 * (bb + 1)],
                            start=False, stop=(k == 15)),
                           reads=[B_wg, B_l2], writes=[B_pp], tag="gumm")
                if which == "g":
                    OP("scalar", lambda e, m=m: e.activation(
                        silu_g, ps_g, AF.Silu, scale=sgup[:, m:m + 1]),
                       reads=[B_psg, B_sgup], writes=[B_silug], tag="silu")
                else:
                    gu_ap, B_gu = GU(m)
                    OP("vector", lambda e, m=m, gu_ap=gu_ap: e.scalar_tensor_tensor(
                        out=gu_ap, in0=ps_u, scalar=sgup[:, 44 + m:45 + m],
                        in1=silu_g, op0=ALU.mult, op1=ALU.mult),
                       reads=[B_psu, B_sgup, B_silug], writes=[B_gu], tag="guev")

        if run_p4:
            colsum_rows(GU, 44, "cs2")

        wdbuf = [MB(32 * KB, 5632), MB(43 * KB, 5632)]
        dstg, B_dstg = MF(180 * KB, 1024)
        ytile, B_yt = MF(184 * KB, 1024)
        ps_d = [PF(0, 1024), PF(1024, 1024)]
        ps_tr = [PF(2048, 512), PF(2560, 512)]
        for m in range(16 if run_p4 else 0):
            wd_ap, B_wd = wdbuf[m % 2]
            OP("gpsimd", lambda e, m=m, wd_ap=wd_ap: e.dma_start(
                out=wd_ap.rearrange("p (k c) -> p k c", k=44),
                in_=d["wdT"][:, 128 * m:128 * (m + 1)].rearrange("(k p) c -> p k c", p=128)),
               writes=[B_wd], dma_slot=f"wd{m % 2}", tag="wddma")
            pp, B_pp = ps_d[m % 2]
            for bb in range(2):
                OP("tensor", lambda e, bb=bb, pp=pp: e.matmul(
                    pp[:, 512 * bb:512 * (bb + 1)], ones_row,
                    cs_hi[:, 512 * bb:512 * (bb + 1)], start=True, stop=False),
                   reads=[B_ones, B_cshi], writes=[B_pp], tag="dseed")
                OP("tensor", lambda e, bb=bb, pp=pp: e.matmul(
                    pp[:, 512 * bb:512 * (bb + 1)], ones_row,
                    cs_lo[:, 512 * bb:512 * (bb + 1)], start=False, stop=False),
                   reads=[B_ones, B_cslo], writes=[B_pp], tag="dseed")
            for k in range(44):
                gu_ap, B_gu = GU(k)
                for bb in range(2):
                    OP("tensor", lambda e, k=k, bb=bb, wd_ap=wd_ap, gu_ap=gu_ap,
                       pp=pp: e.matmul(
                        pp[:, 512 * bb:512 * (bb + 1)],
                        wd_ap[:, 128 * k:128 * (k + 1)],
                        gu_ap[:, 512 * bb:512 * (bb + 1)],
                        start=False, stop=(k == 43)),
                       reads=[B_wd, B_gu], writes=[B_pp], tag="dmm")
            OP("vector", lambda e, m=m, pp=pp: e.tensor_scalar(
                dstg, pp, sdown[:, m:m + 1], None, ALU.mult),
               reads=[B_pp, B_sdown], writes=[B_dstg], tag="dsc")
            for half in range(2):
                pt_ap, B_pt4 = ps_tr[half]
                for j in range(4):
                    tt = 4 * half + j
                    OP("tensor", lambda e, tt=tt, j=j, pt_ap=pt_ap: e.transpose(
                        pt_ap[:, 128 * j:128 * (j + 1)],
                        dstg[:, 128 * tt:128 * (tt + 1)], ident_f),
                       reads=[B_dstg, B_identf], writes=[B_pt4], tag="dtr")
                OP("vector", lambda e, half=half, pt_ap=pt_ap: e.tensor_copy(
                    ytile[:, 512 * half:512 * (half + 1)], pt_ap),
                   reads=[B_pt4], writes=[B_yt], tag="dtrev")
            OP("gpsimd", lambda e, m=m: e.dma_start(
                out=y.rearrange("(t p) c -> p t c", p=128)[:, :, 128 * m:128 * (m + 1)],
                in_=ytile.rearrange("p (t c) -> p t c", t=8),
                accum_op=ALU.add),
               reads=[B_yt, ("dram_y", 0, 1)], writes=[("dram_y", 0, 1)],
               dma_slot="yacc", tag="yacc")

        if "wo" in dbg:
            o = dout("dbg_wo", [128, 16 * 2048], BF16)
            OP("sync", lambda e, o=o: e.dma_start(out=o, in_=wo_all), reads=[B_WO_ALL],
               dma_slot="out")
            o2 = dout("dbg_borep", [128, 2048])
            OP("sync", lambda e, o2=o2: e.dma_start(out=o2, in_=borep), reads=[B_borep],
               dma_slot="out")
        if "bor" in dbg:
            o = dout("dbg_borows", [1, 6144])
            rr_, B_rr = MF(24 * KB, 6144)
            OP("vector", lambda e: e.tensor_copy(rr_[0:1, 0:2048], bo_f),
               reads=[B_bof], writes=[B_rr])
            OP("vector", lambda e: e.tensor_copy(rr_[0:1, 2048:4096], bo_hi),
               reads=[B_bohi], writes=[B_rr])
            OP("vector", lambda e: e.tensor_copy(rr_[0:1, 4096:6144], bo_lo),
               reads=[B_bolo], writes=[B_rr])
            OP("sync", lambda e, o=o: e.dma_start(out=o, in_=rr_[0:1, :]),
               reads=[B_rr], dma_slot="out")
        if "p3s" in dbg:
            cap, B_cap = MB(40 * KB, 2048)
            o = dout("dbg_ms1", [128, 2048], BF16)
            OP("sync", lambda e, o=o, cap=cap: e.dma_start(out=o, in_=cap),
               reads=[B_cap], dma_slot="out")
            cap2, B_cap2 = MF(24 * KB, 2048)
            o2 = dout("dbg_pswo1", [128, 2048])
            OP("sync", lambda e, o2=o2, cap2=cap2: e.dma_start(out=o2, in_=cap2),
               reads=[B_cap2], dma_slot="out")
        if "snap" in dbg:
            o = dout("dbg_snapA", [128, 4096], BF16)
            OP("sync", lambda e, o=o: e.dma_start(out=o, in_=snapA), reads=[B_snapA], dma_slot="out")
            o2 = dout("dbg_snapB", [128, 1536])
            OP("sync", lambda e, o2=o2: e.dma_start(out=o2, in_=snapB), reads=[B_snapB], dma_slot="out")
        if "ck" in dbg:
            o = dout("dbg_ck", [128, 2048])
            xb, B_xb = xst[0]
            OP("vector", lambda e, xb=xb: e.tensor_copy(xb, ck_fm), reads=[B_ckfm], writes=[B_xb])
            OP("sync", lambda e, o=o, xb=xb: e.dma_start(out=o, in_=xb), reads=[B_xb], dma_slot="out")
        if "cv" in dbg:
            o = dout("dbg_cv", [128, 2048])
            xb, B_xb = xst[1]
            OP("vector", lambda e, xb=xb: e.tensor_copy(xb, cv_tm), reads=[B_cvtm], writes=[B_xb])
            OP("sync", lambda e, o=o, xb=xb: e.dma_start(out=o, in_=xb), reads=[B_xb], dma_slot="out")
        if "ent" in dbg:
            o = dout("dbg_ent", [128, 16 * 2048], BF16)
            B_all = ("mega", ENT_LO, ENT_LO + 16 * 4096)
            ap = mega_b[:, ENT_LO // 2: ENT_LO // 2 + 16 * 2048]
            OP("sync", lambda e, o=o, ap=ap: e.dma_start(out=o, in_=ap), reads=[B_all], dma_slot="out")
        if "lnf" in dbg:
            o = dout("dbg_lnf", [128, 16 * NSLOT * 128], BF16)
            ap = mega_b[:, LNF_LO // 2: LNF_LO // 2 + 16 * NSLOT * 128]
            OP("sync", lambda e, o=o, ap=ap: e.dma_start(out=o, in_=ap), reads=[B_LNF_ALL],
               dma_slot="out")
        if "ckrope" in dbg:
            o = dout("dbg_ckpre", [128, 512])
            OP("sync", lambda e, o=o: e.dma_start(out=o, in_=wrap0), reads=[B_wrap0], dma_slot="out")
            o2 = dout("dbg_ckpost", [128, 512])
            OP("sync", lambda e, o2=o2: e.dma_start(out=o2, in_=wrap1), reads=[B_wrap1], dma_slot="out")
        if "lncols" in dbg:
            o = dout("dbg_lncols", [128, 8])
            OP("sync", lambda e, o=o: e.dma_start(out=o, in_=cmax_sb), reads=[B_cmax], dma_slot="out")
        if "lnlast" in dbg:
            o = dout("dbg_lnlast", [128, 2048], BF16)
            OP("sync", lambda e, o=o: e.dma_start(out=o, in_=lnst), reads=[B_lnst], dma_slot="out")
            o2 = dout("dbg_wlsc", [128, 2048], BF16)
            OP("sync", lambda e, o2=o2: e.dma_start(out=o2, in_=wlsc), reads=[B_wlsc], dma_slot="out")
        if "wcmpr" in dbg:
            o = dout("dbg_wcmpr", [128, 2048])
            xb2, B_xb2 = xst[1]
            OP("vector", lambda e, xb2=xb2: e.tensor_copy(xb2, wcmpr), reads=[B_wcmpr], writes=[B_xb2])
            OP("sync", lambda e, o=o, xb2=xb2: e.dma_start(out=o, in_=xb2), reads=[B_xb2], dma_slot="out")
        if "wseq" in dbg:
            o = dout("dbg_wseq", [128, 2048])
            OP("sync", lambda e, o=o: e.dma_start(out=o, in_=wseq), reads=[B_wseq], dma_slot="out")
            o2 = dout("dbg_eseq", [128, 2048])
            OP("sync", lambda e, o2=o2: e.dma_start(out=o2, in_=eseq), reads=[B_eseq], dma_slot="out")
        if "tabs" in dbg:
            o = dout("dbg_tabB", [128, 1024])
            OP("sync", lambda e, o=o: e.dma_start(out=o, in_=tabB), reads=[B_tabB], dma_slot="out")
            o2 = dout("dbg_tabO", [128, 2 * NSLOT * 32])
            OP("sync", lambda e, o2=o2: e.dma_start(out=o2, in_=tabO), reads=[B_tabO],
               dma_slot="out")


        with nc.Block() as block:
            prog.finalize(block)
        for cm in prog._sem_ctxs:
            cm.__exit__(None, None, None)

    return nc, sorted(dbg_outs.keys())


def _prep_inputs(core, raw):
    x = np.ascontiguousarray(np.asarray(raw["L0_attn_hidden_states"],
                                        dtype=np.float32).reshape(S, HID))
    slots = slot_global_tiles(core)
    x_own = np.concatenate([x[128 * g:128 * (g + 1), :] for g in slots], axis=0)
    pos_own = np.concatenate([np.arange(128 * g, 128 * (g + 1), dtype=np.float32)
                              for g in slots])
    posT_own = np.ascontiguousarray(np.broadcast_to(pos_own[None, :],
                                                    (32, NSLOT * 128)))
    own = own_global_tiles(core)
    cmax = np.empty((128, TPC), np.float32)
    p = np.arange(128)
    for j, g in enumerate(own):
        cmax[:, j] = np.floor((128 * g + p - 3) / 4.0).astype(np.float32)
    lm = np.zeros((TPC, 128, 256), np.float32)
    i = np.arange(128)[:, None]
    jj = np.arange(256)[None, :]
    band = (jj >= i) & (jj <= i + 128)
    for t, g in enumerate(own):
        m = band.copy()
        if g == 0:
            m = m & (jj >= 128)
        lm[t] = np.where(m, 0.0, NEG).astype(np.float32)
    lmask = np.ascontiguousarray(lm.transpose(1, 0, 2).reshape(128, TPC * 256))

    inp = {
        "x": x, "x_own": np.ascontiguousarray(x_own), "posT_own": posT_own,
        "cmax": cmax, "lmask": lmask,
        "Wq": np.asarray(raw["L0_attn_Wq"], np.float32),
        "bq": np.asarray(raw["L0_attn_bq"], np.float32),
        "Wcmp": np.asarray(raw["L0_attn_Wcmp"], np.float32),
        "bcmp": np.asarray(raw["L0_attn_bcmp"], np.float32),
        "Wk": np.asarray(raw["L0_attn_Wk"], np.float32),
        "bk": np.asarray(raw["L0_attn_bk"], np.float32),
        "Wv": np.asarray(raw["L0_attn_Wv"], np.float32),
        "bv": np.asarray(raw["L0_attn_bv"], np.float32),
        "Wlk": np.asarray(raw["L0_attn_Wlk"], np.float32),
        "blk": np.asarray(raw["L0_attn_blk"], np.float32),
        "Wlv": np.asarray(raw["L0_attn_Wlv"], np.float32),
        "blv": np.asarray(raw["L0_attn_blv"], np.float32),
        "qn_w": np.asarray(raw["L0_attn_qn_w"], np.float32),
        "kn_w": np.asarray(raw["L0_attn_kn_w"], np.float32),
        "Wo": np.asarray(raw["L0_attn_Wo"], np.float32),
        "bo": np.asarray(raw["L0_attn_bo"], np.float32),
        "wgT": np.ascontiguousarray(np.asarray(raw["L0_ffn_wq_gate"], np.int32).T),
        "s_gate": np.asarray(raw["L0_ffn_s_gate"], np.float32),
        "wuT": np.ascontiguousarray(np.asarray(raw["L0_ffn_wq_up"], np.int32).T),
        "s_up": np.asarray(raw["L0_ffn_s_up"], np.float32),
        "wdT": np.ascontiguousarray(np.asarray(raw["L0_ffn_wq_down"], np.int32).T),
        "s_down": np.asarray(raw["L0_ffn_s_down"], np.float32),
    }
    return inp


def kernel(**raw):
    nc, _ = build_program()
    in_maps = [_prep_inputs(c, raw) for c in range(NCORE)]
    res = run_bass_kernel_spmd(nc, in_maps, core_ids=list(range(NCORE)))
    y = np.zeros((S, HID), np.float32)
    for c in range(NCORE):
        shard = res.results[c]["y"]
        for j, g in enumerate(own_global_tiles(c)):
            y[128 * g:128 * (g + 1), :] = shard[128 * j:128 * (j + 1), :]
    return y.reshape(1, S, HID)
